# version 03
# baseline (speedup 1.0000x reference)
import contextlib
import numpy as np
import ml_dtypes
import concourse.bass as bass
import concourse.mybir as mybir
from concourse.bass_utils import run_bass_kernel_spmd

F32 = mybir.dt.float32
BF16 = mybir.dt.bfloat16
AF = mybir.ActivationFunctionType
ALU = mybir.AluOpType

B, SEQ, H = 4, 4096, 1024
TPC = 2048
NT = 16
NCH = 8
NE, NI = 8, 2048
NU = 16
EPS = 1e-5
SHARDS = ["p", "m", "2"]


class Ctx:

    def __init__(self):
        self.emitting = False
        self.counts = {}
        self.marks = {}
        self.sems = {}
        self._waited = {}

    def begin(self, emitting):
        self.emitting = emitting
        self.counts = {}
        self._waited = {}

    def inc(self, inst, sem, n=1, mark=None):
        c = self.counts.get(sem, 0) + n
        self.counts[sem] = c
        if self.emitting:
            assert inst is not None, f"missing inst for inc {sem} {mark}"
            inst.then_inc(self.sems[sem], n)
            if mark is not None:
                assert self.marks[mark] == (sem, c), (mark, self.marks[mark], sem, c)
        else:
            if mark is not None:
                assert mark not in self.marks, mark
                self.marks[mark] = (sem, c)
        return c

    def alias(self, new, old):
        if not self.emitting:
            self.marks[new] = self.marks[old]

    def fence(self, inst, eng, sem):
        c = self.inc(inst, sem)
        if self.emitting:
            eng.wait_ge(self.sems[sem], c)

    def wait(self, eng, ename, mark):
        if not self.emitting:
            return
        sem, v = self.marks[mark]
        self._do_wait(eng, ename, sem, v)

    def wait_q(self, eng, ename, sem, v):
        if not self.emitting:
            return
        self._do_wait(eng, ename, sem, v)

    def _do_wait(self, eng, ename, sem, v):
        if v <= 0:
            return
        key = (ename, sem)
        if self._waited.get(key, -1) >= v:
            return
        self._waited[key] = v
        eng.wait_ge(self.sems[sem], v)


def build_nc(dbg=False):
    nc = bass.Bass("TRN2", target_bir_lowering=False, debug=False,
                   detect_race_conditions=False)

    d_x = nc.dram_tensor("x", [TPC, H], F32, kind="ExternalInput").ap()
    d_xp = nc.dram_tensor("xp", [TPC, H], F32, kind="ExternalInput").ap()
    d_flag = nc.dram_tensor("flag", [128, 1], F32, kind="ExternalInput").ap()
    d_ccm = nc.dram_tensor("c_cm", [128, 8], F32, kind="ExternalInput").ap()
    d_td = nc.dram_tensor("td_cm", [128, 8], F32, kind="ExternalInput").ap()
    d_tmv = nc.dram_tensor("tmv_cm", [128, 8], F32, kind="ExternalInput").ap()
    d_tmr = nc.dram_tensor("tmr_cm", [128, 8], F32, kind="ExternalInput").ap()
    d_wmod = nc.dram_tensor("wmod", [H, 6 * H], F32, kind="ExternalInput").ap()
    d_WvT = nc.dram_tensor("WvT", [H, H], F32, kind="ExternalInput").ap()
    d_WrT = nc.dram_tensor("WrT", [H, H], F32, kind="ExternalInput").ap()
    d_WoT = nc.dram_tensor("WoT", [H, H], F32, kind="ExternalInput").ap()
    d_Wg = nc.dram_tensor("Wg_l", [128, 64], F32, kind="ExternalInput").ap()
    d_W1 = nc.dram_tensor("W1", [NE, H, NI], F32, kind="ExternalInput").ap()
    d_b1 = nc.dram_tensor("b1_cm", [128, 128], F32, kind="ExternalInput").ap()
    d_W2 = nc.dram_tensor("W2", [NE, NI, H], F32, kind="ExternalInput").ap()
    d_b2 = nc.dram_tensor("b2", [8, H], F32, kind="ExternalInput").ap()
    d_idf = nc.dram_tensor("identf", [128, 128], F32, kind="ExternalInput").ap()
    d_idb = nc.dram_tensor("identb", [128, 128], BF16, kind="ExternalInput").ap()
    d_ones = nc.dram_tensor("onescol", [1, 128], BF16, kind="ExternalInput").ap()
    d_out = nc.dram_tensor("out", [TPC, H], F32, kind="ExternalOutput").ap()
    d_dbg = (nc.dram_tensor("dbg", [128, 16640], F32, kind="ExternalOutput").ap()
             if dbg else None)

    ctx = Ctx()

    with contextlib.ExitStack() as stk:
        E = stk.enter_context

        xt = E(nc.sbuf_tensor("xt", [128, 16384], F32))
        ARW = E(nc.sbuf_tensor("ARW", [128, 16384], F32))
        WA = E(nc.sbuf_tensor("WA", [128, 16384], BF16))
        SCR = E(nc.sbuf_tensor("SCR", [128, 17408], BF16))
        gb_bc = E(nc.sbuf_tensor("gb_bc", [128, 1024], BF16))
        xnscr = E(nc.sbuf_tensor("xnscr", [128, 2048], BF16))
        cvec = E(nc.sbuf_tensor("cvec", [128, 8], F32))
        tdc = E(nc.sbuf_tensor("tdc", [128, 8], F32))
        tmv = E(nc.sbuf_tensor("tmv", [128, 8], F32))
        tmr = E(nc.sbuf_tensor("tmr", [128, 8], F32))
        tmv1 = E(nc.sbuf_tensor("tmv1", [128, 8], F32))
        tmr1 = E(nc.sbuf_tensor("tmr1", [128, 8], F32))
        dcol = E(nc.sbuf_tensor("dcol", [128, 8], F32))
        dcolb = E(nc.sbuf_tensor("dcolb", [128, 8], BF16))
        flagc = E(nc.sbuf_tensor("flagc", [128, 1], F32))
        silu_f = E(nc.sbuf_tensor("silu_f", [128, 8], F32))
        silu_b = E(nc.sbuf_tensor("silu_b", [128, 8], BF16))
        carryP = E(nc.sbuf_tensor("carryP", [128, 8], F32))
        modc = E(nc.sbuf_tensor("modc", [128, 48], F32))
        mscr = E(nc.sbuf_tensor("mscr", [128, 16], F32))
        garow2 = E(nc.sbuf_tensor("garow2", [1, 2048], BF16))
        ssum = E(nc.sbuf_tensor("ssum", [128, 16], F32))
        ssq = E(nc.sbuf_tensor("ssq", [128, 16], F32))
        sds = E(nc.sbuf_tensor("sds", [128, 16], F32))
        srstd = E(nc.sbuf_tensor("srstd", [128, 16], F32))
        snmr = E(nc.sbuf_tensor("snmr", [128, 16], F32))
        wg_sb = E(nc.sbuf_tensor("wg_sb", [128, 64], BF16))
        identf = E(nc.sbuf_tensor("idf_s", [128, 128], F32))
        identb = E(nc.sbuf_tensor("idb_s", [128, 128], BF16))
        ones_c = E(nc.sbuf_tensor("ones_c", [1, 128], BF16))
        lg = E(nc.sbuf_tensor("lg", [128, 8], F32))
        gsc = E(nc.sbuf_tensor("gsc", [128, 8], F32))
        gc1 = E(nc.sbuf_tensor("gc1", [128, 4], F32))
        wt = E(nc.sbuf_tensor("wt", [128, 128], F32))

        PA = [E(nc.psum_tensor(f"PA{i}", [128, 1024], F32)) for i in range(4)]
        PAb = [p[:].bitcast(BF16) for p in PA]

        for s in ["yS", "yX", "yP", "yL", "pW0", "pW1", "pG", "pR",
                  "pWr", "pWv", "pWo", "pU0", "pU1",
                  "sT", "sS", "sV", "sO", "fV", "fS"]:
            ctx.sems[s] = E(nc.semaphore(s))

        ARWb = ARW[:].bitcast(BF16)
        vin = ARWb[:, 0:16384]
        rin = ARWb[:, 16384:32768]
        v_scr = ARWb[:, 16384:24576]
        uslot = [
            (ARWb[:, 24576:32768], ARWb[:, 16384:24576]),
            (ARWb[:, 0:8192], ARWb[:, 8192:16384]),
        ]
        xp_f = ARW

        wmod_sl = [WA[:, 0:6144], WA[:, 6144:12288]]
        WrT_s = WA[:, 0:8192]
        WvT_s = WA[:, 8192:16384]
        WoT_s = WA[:, 0:8192]
        h1s = WA[:, 0:8192]
        b1c = WA[:, 8192:8448].bitcast(F32)
        b2_sb = WA[0:8, 8448:10496].bitcast(F32)
        b2g = WA[0:8, 10496:11520]
        wTb = WA[0:8, 11520:13568]

        r_sig = SCR[:, 0:16384]
        h_scr = [SCR[:, 0:2048], SCR[:, 2048:4096]]
        p_scr = [SCR[:, 4096:6144], SCR[:, 6144:8192]]
        t1s = SCR[:, 8192:10240]
        t2s = SCR[:, 10240:12288]
        sqd = SCR[:, 12288:14336].bitcast(F32)
        h_bT = SCR[:, 0:16384]
        tmpb = SCR[:, 16384:17408]
        ga_bc = tmpb

        def src_tile(sh, i):
            return (xp_f if sh == "p" else xt)[:, i * 1024:(i + 1) * 1024]

        def gidx(sh, c, g):
            return (SHARDS.index(sh) * NCH + c) * 2 + g

        def emit_Y(eng):
            em = ctx.emitting
            dmas = [
                (cvec[:], d_ccm), (tdc[:], d_td), (tmv[:], d_tmv), (tmr[:], d_tmr),
                (flagc[:], d_flag), (identf[:], d_idf), (identb[:], d_idb),
                (ones_c[:], d_ones),
            ]
            for k in range(4):
                dmas.append((
                    xt[:, k * 4096:(k + 1) * 4096].rearrange("p (i f) -> p i f", f=1024),
                    d_x[k * 512:(k + 1) * 512, :].rearrange("(i p) f -> p i f", p=128)))
            for k in range(4):
                dmas.append((
                    xp_f[:, k * 4096:(k + 1) * 4096].rearrange("p (i f) -> p i f", f=1024),
                    d_xp[k * 512:(k + 1) * 512, :].rearrange("(i p) f -> p i f", p=128)))
            for k, (dst, src) in enumerate(dmas):
                sem = "yS" if k < 8 else ("yX" if k < 12 else "yP")
                i = None
                if em:
                    i = eng.dma_start(out=dst, in_=src)
                ctx.inc(i, sem, 16)
            ctx.wait(eng, "Y", ("wv_done",))
            for dst, src in [(b1c, d_b1), (b2_sb, d_b2)]:
                i = None
                if em:
                    i = eng.dma_start(out=dst, in_=src)
                ctx.inc(i, "yL", 16)
            for g in range(16):
                ctx.wait(eng, "Y", ("evac", 15, g // 8, g % 8))
                i = None
                if em:
                    i = eng.dma_start(out=d_out[g * 128:(g + 1) * 128, :],
                                      in_=xt[:, g * 1024:(g + 1) * 1024])
                ctx.inc(i, "sO", 16)
            if dbg:
                for dst, srcap in [(d_dbg[:, 0:48], modc[:]),
                                   (d_dbg[:, 48:56], carryP[:]),
                                   (d_dbg[:, 56:184], wt[:]),
                                   (d_dbg[:, 256:8448], SCR[:, 0:16384].bitcast(F32))]:
                    i = None
                    if em:
                        i = eng.dma_start(out=dst, in_=srcap)
                    ctx.inc(i, "sO", 16)
            ctx.wait_q(eng, "Y", "sO", 16 * 16 + (64 if dbg else 0))

        def emit_P(eng):
            em = ctx.emitting

            def dma(sem, dst, src):
                i = None
                if em:
                    i = eng.dma_start(out=dst, in_=src)
                ctx.inc(i, sem, 16)

            for hc in range(8):
                if hc >= 2:
                    ctx.wait(eng, "P", ("modchunk", hc - 2))
                dma("pW%d" % (hc % 2), wmod_sl[hc % 2], d_wmod[hc * 128:(hc + 1) * 128, :])
            dma("pG", wg_sb[:], d_Wg)
            ctx.wait(eng, "P", ("modc_tf",))
            for cs, off in [(16, 0), (40, 1024)]:
                for j in range(8):
                    dma("pR", garow2[0:1, off + j * 128:off + (j + 1) * 128],
                        modc[:, cs + j:cs + j + 1])
            dma("pWr", WrT_s.rearrange("p (c f) -> p c f", f=1024),
                d_WrT.rearrange("(c p) f -> p c f", p=128))
            dma("pWv", WvT_s.rearrange("p (c f) -> p c f", f=1024),
                d_WvT.rearrange("(c p) f -> p c f", p=128))
            ctx.wait(eng, "P", ("wr_done",))
            dma("pWo", WoT_s.rearrange("p (c f) -> p c f", f=1024),
                d_WoT.rearrange("(c p) f -> p c f", p=128))
            for u in range(NU):
                e, ih = u // 2, u % 2
                w1d, w2d = uslot[u % 2]
                if u == 1:
                    ctx.wait(eng, "P", ("wv_done",))
                elif u >= 2:
                    ctx.wait(eng, "P", ("w2", u - 2, 1, 7))
                dma("pU%d" % (u % 2), w1d.rearrange("p (c f) -> p c f", f=1024),
                    d_W1[e, :, ih * 1024:(ih + 1) * 1024].rearrange("(c p) f -> p c f", p=128))
                if u == 0:
                    ctx.wait(eng, "P", ("wo_done",))
                dma("pU%d" % (u % 2), w2d.rearrange("p (c f) -> p c f", f=1024),
                    d_W2[e, ih * 1024:(ih + 1) * 1024, :].rearrange("(c p) f -> p c f", p=128))

        def emit_T(eng):
            em = ctx.emitting

            ctx.wait(eng, "T", ("silu",))
            for hc in range(8):
                ctx.wait_q(eng, "T", "pW%d" % (hc % 2), 16 * (hc // 2 + 1))
                i = None
                for oc in range(48):
                    if em:
                        i = eng.matmul(PA[3][:, oc:oc + 1],
                                       wmod_sl[hc % 2][:, oc * 128:(oc + 1) * 128],
                                       silu_b[:, hc:hc + 1],
                                       start=(hc == 0 and oc == 0),
                                       stop=(hc == 7 and oc == 47),
                                       skip_group_check=True)
                ctx.inc(i, "sT", mark=("modchunk", hc))
            ctx.alias(("mod_done",), ("modchunk", 7))

            ctx.wait_q(eng, "T", "pR", 256)
            i = None
            if em:
                for j in range(8):
                    eng.matmul(PA[2][:, j * 128:(j + 1) * 128], ones_c[:],
                               garow2[0:1, j * 128:(j + 1) * 128], start=True, stop=True)
                for j in range(8):
                    i = eng.matmul(PA[1][:, j * 128:(j + 1) * 128], ones_c[:],
                                   garow2[0:1, 1024 + j * 128:1024 + (j + 1) * 128],
                                   start=True, stop=True)
            ctx.inc(i, "sT", mark=("gabc_ps",))

            ctx.wait_q(eng, "T", "yS", 128)
            for sh in SHARDS:
                for c in range(NCH):
                    for g in range(2):
                        gi = gidx(sh, c, g)
                        ctx.wait(eng, "T", ("xn", sh, c, g))
                        if gi >= 2:
                            pg = gi - 2
                            psh, pr = divmod(pg, NCH * 2)
                            ctx.wait(eng, "T", ("trev", SHARDS[psh], pr // 2, pr % 2))
                        else:
                            ctx.wait(eng, "T", ("gabc",))
                        i = None
                        if em:
                            for j in range(8):
                                i = eng.transpose(
                                    PAb[g % 2][:, j * 128:(j + 1) * 128],
                                    xnscr[:, (g % 2) * 1024 + j * 128:(g % 2) * 1024 + (j + 1) * 128],
                                    identb[:])
                        ctx.inc(i, "sT", mark=("tr", sh, c, g))
                if sh == "m":
                    emit_T_attn(eng)
                elif sh == "2":
                    emit_T_gate_ffn(eng)

        def emit_T_attn(eng):
            em = ctx.emitting
            ctx.wait(eng, "T", ("mix", 7))
            ctx.wait_q(eng, "T", "pWr", 16)
            for tt in range(NT):
                if tt >= 2:
                    ctx.wait(eng, "T", ("sigm", tt - 2))
                i = None
                if em:
                    for hc in range(8):
                        for nh in range(2):
                            i = eng.matmul(
                                PA[tt % 2][:, nh * 512:(nh + 1) * 512],
                                rin[:, hc * 2048 + tt * 128:hc * 2048 + (tt + 1) * 128],
                                WrT_s[:, hc * 1024 + nh * 512:hc * 1024 + (nh + 1) * 512],
                                start=(hc == 0), stop=(hc == 7))
                ctx.inc(i, "sT", mark=("wr", tt))
            ctx.alias(("wr_done",), ("wr", NT - 1))

            ctx.wait_q(eng, "T", "pWv", 16)
            ctx.wait(eng, "T", ("wotg",))
            for t4 in range(4):
                for oc in range(8):
                    gi = t4 * 8 + oc
                    if gi >= 2:
                        ctx.wait(eng, "T", ("vev", (gi - 2) // 8, (gi - 2) % 8))
                    i = None
                    if em:
                        for hc in range(8):
                            i = eng.matmul(
                                PA[2][:, (oc % 2) * 512:(oc % 2 + 1) * 512],
                                WvT_s[:, hc * 1024 + oc * 128:hc * 1024 + (oc + 1) * 128],
                                vin[:, hc * 2048 + t4 * 512:hc * 2048 + (t4 + 1) * 512],
                                start=(hc == 0), stop=(hc == 7))
                    ctx.inc(i, "sT", mark=("wv", t4, oc))
                if t4 == 3:
                    ctx.alias(("wv_done",), ("wv", 3, 7))
                for ti in range(4):
                    tt = t4 * 4 + ti
                    ctx.wait(eng, "T", ("vev", t4, 7))
                    if tt >= 2:
                        ctx.wait(eng, "T", ("ep", tt - 2))
                    else:
                        ctx.wait(eng, "T", ("sigm", NT - 2 + tt))
                    i = None
                    if em:
                        for oc in range(8):
                            for nh in range(2):
                                i = eng.matmul(
                                    PA[tt % 2][:, nh * 512:(nh + 1) * 512],
                                    v_scr[:, oc * 512 + ti * 128:oc * 512 + (ti + 1) * 128],
                                    WoT_s[:, oc * 1024 + nh * 512:oc * 1024 + (nh + 1) * 512],
                                    start=(oc == 0), stop=(oc == 7))
                    ctx.inc(i, "sT", mark=("wo", tt))
            ctx.alias(("wo_done",), ("wo", NT - 1))

        def emit_T_gate_ffn(eng):
            em = ctx.emitting
            ctx.wait(eng, "T", ("trev", "2", NCH - 1, 1))
            ctx.wait_q(eng, "T", "pG", 16)
            for tt in range(NT):
                if tt >= 2:
                    ctx.wait(eng, "T", ("wTb", tt - 2))
                lgc = (tt % 2) * 512
                i = None
                if em:
                    for hc in range(8):
                        i = eng.matmul(
                            PA[3][:, lgc:lgc + 8],
                            h_bT[:, hc * 2048 + tt * 128:hc * 2048 + (tt + 1) * 128],
                            wg_sb[:, hc * 8:(hc + 1) * 8],
                            start=(hc == 0), stop=(hc == 7))
                ctx.inc(i, "sT", mark=("lg", tt))
                ctx.wait(eng, "T", ("w", tt))
                i = None
                if em:
                    i = eng.transpose(PA[3][0:8, lgc + 128:lgc + 256],
                                      wt[:, tt * 8:(tt + 1) * 8], identf[:])
                ctx.inc(i, "sT", mark=("wtr", tt))

            for u in range(NU):
                w1s, w2s = uslot[u % 2]
                ctx.wait(eng, "T", ("ufold", u))
                for tc in range(2):
                    for ic in range(8):
                        if ic >= 2:
                            ctx.wait(eng, "T", ("gelu", u, tc, ic - 2))
                        elif tc > 0 or u > 0:
                            pu, ptc = (u, tc - 1) if tc > 0 else (u - 1, 1)
                            ctx.wait(eng, "T", ("gelu", pu, ptc, ic + 6))
                        i = None
                        if em:
                            for hc in range(8):
                                for nh in range(2):
                                    i = eng.matmul(
                                        PA[ic % 2][:, nh * 512:(nh + 1) * 512],
                                        w1s[:, hc * 1024 + ic * 128:hc * 1024 + (ic + 1) * 128],
                                        h_bT[:, hc * 2048 + tc * 1024 + nh * 512:
                                             hc * 2048 + tc * 1024 + (nh + 1) * 512],
                                        start=(hc == 0), stop=(hc == 7))
                        ctx.inc(i, "sT", mark=("h1", u, tc, ic))
                    ctx.wait(eng, "T", ("gelu", u, tc, 7))
                    for ts in range(8):
                        w2g = (u * 2 + tc) * 8 + ts
                        if w2g >= 2:
                            pg = w2g - 2
                            ctx.wait(eng, "T", ("evac", pg // 16, (pg % 16) // 8, pg % 8))
                        i = None
                        if em:
                            if u == 0:
                                for nh in range(2):
                                    eng.matmul(
                                        PA[2 + ts % 2][:, nh * 512:(nh + 1) * 512],
                                        wTb[:, (tc * 8 + ts) * 128:(tc * 8 + ts + 1) * 128],
                                        b2g[:, nh * 512:(nh + 1) * 512],
                                        start=True, stop=False)
                            for ic in range(8):
                                for nh in range(2):
                                    i = eng.matmul(
                                        PA[2 + ts % 2][:, nh * 512:(nh + 1) * 512],
                                        h1s[:, ic * 1024 + ts * 128:ic * 1024 + (ts + 1) * 128],
                                        w2s[:, ic * 1024 + nh * 512:ic * 1024 + (nh + 1) * 512],
                                        start=(u != 0 and ic == 0), stop=(ic == 7))
                        ctx.inc(i, "sT", mark=("w2", u, tc, ts))

        def emit_V(eng):
            em = ctx.emitting
            ctx.wait_q(eng, "V", "yS", 128)
            if em:
                eng.tensor_scalar(tmv1[:], tmv[:], -1.0, 1.0, ALU.mult, ALU.add)
                eng.tensor_scalar(tmr1[:], tmr[:], -1.0, 1.0, ALU.mult, ALU.add)

            ctx.wait(eng, "V", ("mod_done",))
            i = None
            ctx.fence(eng.tensor_copy(modc[:], PA[3][:, 0:48]) if em else None, eng, "fV")
            if em:
                eng.tensor_scalar(modc[:, 8:16], modc[:, 8:16], 1.0, None, ALU.add)
                eng.tensor_scalar(modc[:, 32:40], modc[:, 32:40], 1.0, None, ALU.add)
            for cs in (16, 40):
                ctx.fence(eng.tensor_scalar(mscr[:, 0:8], modc[:, cs:cs + 8], -1.0, None, ALU.mult) if em else None, eng, "fV")
                ctx.fence(eng.tensor_tensor(mscr[:, 0:8], modc[:, cs:cs + 8], mscr[:, 0:8], ALU.max) if em else None, eng, "fV")
                ctx.fence(eng.tensor_scalar(mscr[:, 0:8], mscr[:, 0:8], 0.25, None, ALU.max) if em else None, eng, "fV")
                ctx.fence(eng.tensor_scalar(mscr[:, 8:16], modc[:, cs:cs + 8], 0.0, None, ALU.is_ge) if em else None, eng, "fV")
                ctx.fence(eng.tensor_scalar(mscr[:, 8:16], mscr[:, 8:16], 2.0, -1.0, ALU.mult, ALU.add) if em else None, eng, "fV")
                i = eng.tensor_tensor(modc[:, cs:cs + 8], mscr[:, 0:8], mscr[:, 8:16], ALU.mult) if em else None
            ctx.inc(i, "sV", mark=("modc_tf",))
            ctx.wait(eng, "V", ("gabc_ps",))
            i = None
            if em:
                eng.tensor_copy(ga_bc[:, 0:1024], PA[2][:, 0:1024])
                i = eng.tensor_copy(gb_bc[:], PA[1][:, 0:1024])
            ctx.inc(i, "sV", mark=("gabc",))

            for sh in ["p", "m"]:
                ctx.wait_q(eng, "V", "yP" if sh == "p" else "yX", 64)
                for i2 in range(NT):
                    if em:
                        eng.reduce_sum(ssum[:, i2:i2 + 1], src_tile(sh, i2),
                                       mybir.AxisListType.X)
                ctx.wait(eng, "V", ("sq", sh, NT - 1))
                ctx.fence(eng.tensor_scalar(ssum[:], ssum[:], 1.0 / 1024, None, ALU.mult) if em else None, eng, "fV")
                ctx.fence(eng.tensor_scalar(ssq[:], ssq[:], 1.0 / 1024, None, ALU.mult) if em else None, eng, "fV")
                ctx.fence(eng.tensor_tensor(snmr[:], ssum[:], ssum[:], ALU.mult) if em else None, eng, "fV")
                ctx.fence(eng.tensor_tensor(ssq[:], ssq[:], snmr[:], ALU.subtract) if em else None, eng, "fV")
                i = eng.tensor_scalar(ssq[:], ssq[:], EPS, None, ALU.add) if em else None
                ctx.inc(i, "sV", mark=("varred", sh))
                ctx.wait(eng, "V", ("sd", sh))
                ctx.fence(eng.reciprocal(srstd[:], sds[:]) if em else None, eng, "fV")
                ctx.fence(eng.tensor_tensor(snmr[:], ssum[:], srstd[:], ALU.mult) if em else None, eng, "fV")
                i = eng.tensor_scalar(snmr[:], snmr[:], -1.0, None, ALU.mult) if em else None
                ctx.inc(i, "sV", mark=("stat", sh))
                ctx.fence(None if not em else eng.tensor_copy(gc1[:, 0:1], snmr[:, 0:1]), eng, "fV")

                for c in range(NCH):
                    for g in range(2):
                        gi = gidx(sh, c, g)
                        if gi >= 2:
                            pg = gi - 2
                            psh, pr = divmod(pg, NCH * 2)
                            ctx.wait(eng, "V", ("tr", SHARDS[psh], pr // 2, pr % 2))
                        i = None
                        if em:
                            for j in range(8):
                                ti = g * 8 + j
                                i = eng.tensor_scalar(
                                    xnscr[:, (g % 2) * 1024 + j * 128:(g % 2) * 1024 + (j + 1) * 128],
                                    src_tile(sh, ti)[:, c * 128:(c + 1) * 128],
                                    srstd[:, ti:ti + 1], snmr[:, ti:ti + 1],
                                    ALU.mult, ALU.add)
                        ctx.inc(i, "sV", mark=("xn", sh, c, g))
                        ctx.wait(eng, "V", ("tr", sh, c, g))
                        ccol, scol = (8, 0)
                        i = None
                        if em:
                            i = eng.tensor_scalar(
                                h_scr[c % 2][:, g * 1024:(g + 1) * 1024],
                                PAb[g % 2][:, 0:1024],
                                modc[:, ccol + c:ccol + c + 1],
                                modc[:, scol + c:scol + c + 1],
                                ALU.mult, ALU.add)
                        ctx.inc(i, "sV", mark=("trev", sh, c, g))
                    if sh == "m" and c >= 2:
                        ctx.wait(eng, "V", ("mixS", c - 2))
                    ctx.wait(eng, "V", ("d",))
                    i = None
                    if em:
                        if sh == "m":
                            eng.tensor_copy(p_scr[c % 2][:, 0:1], carryP[:, c:c + 1])
                            i = eng.tensor_tensor_scan(
                                p_scr[c % 2][:, 1:2048],
                                dcolb[:, c:c + 1].broadcast_to([128, 2047]),
                                h_scr[c % 2][:, 0:2047],
                                carryP[:, c:c + 1], ALU.mult, ALU.add)
                        else:
                            i = eng.tensor_tensor_scan(
                                p_scr[c % 2][:, 0:2048],
                                dcolb[:, c:c + 1].broadcast_to([128, 2048]),
                                h_scr[c % 2][:, 0:2048],
                                0.0, ALU.mult, ALU.add)
                    ctx.inc(i, "sV", mark=("scan", sh, c))
                    if sh == "p":
                        ctx.wait_q(eng, "V", "yS", 128)
                        ctx.fence(eng.engine_nop() if em else None, eng, "fV")
                        i = None
                        if em:
                            i = eng.tensor_scalar(carryP[:, c:c + 1],
                                                  p_scr[c % 2][:, 2047:2048],
                                                  flagc[:, 0:1], None, ALU.mult)
                        ctx.inc(i, "sV", mark=("carry", c))
                    else:
                        ctx.wait(eng, "V", ("mixS", c))
                        i = None
                        if em:
                            eng.scalar_tensor_tensor(
                                vin[:, c * 2048:(c + 1) * 2048], h_scr[c % 2],
                                tmv[:, c:c + 1], t1s[:], ALU.mult, ALU.add)
                            i = eng.scalar_tensor_tensor(
                                rin[:, c * 2048:(c + 1) * 2048], h_scr[c % 2],
                                tmr[:, c:c + 1], t2s[:], ALU.mult, ALU.add)
                        ctx.inc(i, "sV", mark=("mix", c))

            ctx.wait_q(eng, "V", "pWo", 16)
            i = None
            if em:
                for k in range(8):
                    i = eng.tensor_tensor(WoT_s[:, k * 1024:(k + 1) * 1024],
                                          WoT_s[:, k * 1024:(k + 1) * 1024],
                                          ga_bc[:, 0:1024], ALU.mult)
            ctx.inc(i, "sV", mark=("wotg",))

            for t4 in range(4):
                for oc in range(8):
                    ctx.wait(eng, "V", ("wv", t4, oc))
                    i = None
                    if em:
                        i = eng.tensor_copy(v_scr[:, oc * 512:(oc + 1) * 512],
                                            PA[2][:, (oc % 2) * 512:(oc % 2 + 1) * 512])
                    ctx.inc(i, "sV", mark=("vev", t4, oc))
                for ti in range(4):
                    tt = t4 * 4 + ti
                    ctx.wait(eng, "V", ("wo", tt))
                    ctx.wait(eng, "V", ("sigm", tt))
                    i = None
                    if em:
                        eng.tensor_tensor(tmpb[:, 0:1024], PA[tt % 2][:, 0:1024],
                                          r_sig[:, tt * 1024:(tt + 1) * 1024], ALU.mult)
                        i = eng.tensor_tensor(xt[:, tt * 1024:(tt + 1) * 1024],
                                              xt[:, tt * 1024:(tt + 1) * 1024],
                                              tmpb[:, 0:1024], ALU.add)
                    ctx.inc(i, "sV", mark=("ep", tt))

            for i2 in range(NT):
                if em:
                    eng.reduce_sum(ssum[:, i2:i2 + 1], xt[:, i2 * 1024:(i2 + 1) * 1024],
                                   mybir.AxisListType.X)
            ctx.wait(eng, "V", ("sq", "2", NT - 1))
            ctx.fence(eng.tensor_scalar(ssum[:], ssum[:], 1.0 / 1024, None, ALU.mult) if em else None, eng, "fV")
            ctx.fence(eng.tensor_scalar(ssq[:], ssq[:], 1.0 / 1024, None, ALU.mult) if em else None, eng, "fV")
            ctx.fence(eng.tensor_tensor(snmr[:], ssum[:], ssum[:], ALU.mult) if em else None, eng, "fV")
            ctx.fence(eng.tensor_tensor(ssq[:], ssq[:], snmr[:], ALU.subtract) if em else None, eng, "fV")
            i = eng.tensor_scalar(ssq[:], ssq[:], EPS, None, ALU.add) if em else None
            ctx.inc(i, "sV", mark=("varred", "2"))
            ctx.wait(eng, "V", ("sd", "2"))
            ctx.fence(eng.reciprocal(srstd[:], sds[:]) if em else None, eng, "fV")
            ctx.fence(eng.tensor_tensor(snmr[:], ssum[:], srstd[:], ALU.mult) if em else None, eng, "fV")
            i = eng.tensor_scalar(snmr[:], snmr[:], -1.0, None, ALU.mult) if em else None
            ctx.inc(i, "sV", mark=("stat", "2"))
            ctx.fence(None if not em else eng.tensor_copy(gc1[:, 0:1], snmr[:, 0:1]), eng, "fV")
            for c in range(NCH):
                for g in range(2):
                    gi = gidx("2", c, g)
                    pg = gi - 2
                    psh, pr = divmod(pg, NCH * 2)
                    ctx.wait(eng, "V", ("tr", SHARDS[psh], pr // 2, pr % 2))
                    i = None
                    if em:
                        for j in range(8):
                            ti = g * 8 + j
                            i = eng.tensor_scalar(
                                xnscr[:, (g % 2) * 1024 + j * 128:(g % 2) * 1024 + (j + 1) * 128],
                                xt[:, ti * 1024 + c * 128:ti * 1024 + c * 128 + 128],
                                srstd[:, ti:ti + 1], snmr[:, ti:ti + 1],
                                ALU.mult, ALU.add)
                    ctx.inc(i, "sV", mark=("xn", "2", c, g))
                    ctx.wait(eng, "V", ("tr", "2", c, g))
                    i = None
                    if em:
                        i = eng.tensor_scalar(
                            h_bT[:, c * 2048 + g * 1024:c * 2048 + (g + 1) * 1024],
                            PAb[g % 2][:, 0:1024],
                            modc[:, 32 + c:32 + c + 1], modc[:, 24 + c:24 + c + 1],
                            ALU.mult, ALU.add)
                    ctx.inc(i, "sV", mark=("trev", "2", c, g))

            ctx.wait_q(eng, "V", "yL", 32)
            i = None
            if em:
                i = eng.tensor_tensor(b2g[:], b2_sb, gb_bc[0:8, 0:1024], ALU.mult)
            ctx.inc(i, "sV", mark=("b2g",))
            for tt in range(NT):
                ctx.wait(eng, "V", ("lg", tt))
                lgc = (tt % 2) * 512
                ctx.fence(eng.tensor_copy(lg[:], PA[3][:, lgc:lgc + 8]) if em else None, eng, "fV")
                ctx.fence(eng.reduce_max(gc1[:, 0:1], lg[:], mybir.AxisListType.X) if em else None, eng, "fV")
                ctx.fence(eng.tensor_scalar(gc1[:, 1:2], gc1[:, 0:1], -1.0, None, ALU.mult) if em else None, eng, "fV")
                ctx.fence(eng.tensor_scalar(gsc[:], lg[:], gc1[:, 0:1], None, ALU.is_equal) if em else None, eng, "fV")
                ctx.fence(eng.scalar_tensor_tensor(gsc[:], gsc[:], -1e30, lg[:], ALU.mult, ALU.add) if em else None, eng, "fV")
                ctx.fence(eng.reduce_max(gc1[:, 2:3], gsc[:], mybir.AxisListType.X) if em else None, eng, "fV")
                i = eng.tensor_scalar(mscr[:, 0:8], lg[:], gc1[:, 2:3], None, ALU.is_ge) if em else None
                ctx.inc(i, "sV", mark=("lgprep", tt))
                ctx.wait(eng, "V", ("gexp", tt))
                ctx.fence(eng.tensor_scalar(gc1[:, 3:4], gc1[:, 3:4], 1.0, None, ALU.add) if em else None, eng, "fV")
                ctx.fence(eng.reciprocal(gc1[:, 3:4], gc1[:, 3:4]) if em else None, eng, "fV")
                ctx.fence(eng.tensor_tensor(wt[:, tt * 8:(tt + 1) * 8], mscr[:, 0:8], lg[:], ALU.mult) if em else None, eng, "fV")
                i = eng.tensor_scalar(wt[:, tt * 8:(tt + 1) * 8],
                                      wt[:, tt * 8:(tt + 1) * 8],
                                      gc1[:, 3:4], None, ALU.mult) if em else None
                ctx.inc(i, "sV", mark=("w", tt))
                ctx.wait(eng, "V", ("wtr", tt))
                lgc = (tt % 2) * 512
                i = None
                if em:
                    i = eng.tensor_copy(wTb[:, tt * 128:(tt + 1) * 128],
                                        PA[3][0:8, lgc + 128:lgc + 256])
                ctx.inc(i, "sV", mark=("wTb", tt))

            for u in range(NU):
                w1s, w2s = uslot[u % 2]
                ctx.wait_q(eng, "V", "pU%d" % (u % 2), 32 * (u // 2 + 1))
                i = None
                if em:
                    for k in range(8):
                        i = eng.tensor_tensor(w2s[:, k * 1024:(k + 1) * 1024],
                                              w2s[:, k * 1024:(k + 1) * 1024],
                                              gb_bc[:, 0:1024], ALU.mult)
                ctx.inc(i, "sV", mark=("ufold", u))
                e = u // 2
                for tc in range(2):
                    for ts in range(8):
                        ctx.wait(eng, "V", ("w2", u, tc, ts))
                        g = tc * 8 + ts
                        i = None
                        if em:
                            i = eng.scalar_tensor_tensor(
                                xt[:, g * 1024:(g + 1) * 1024],
                                PA[2 + ts % 2][:, 0:1024],
                                wt[:, g * 8 + e:g * 8 + e + 1],
                                xt[:, g * 1024:(g + 1) * 1024],
                                ALU.mult, ALU.add)
                        ctx.inc(i, "sV", mark=("evac", u, tc, ts))

        def emit_S(eng):
            em = ctx.emitting
            ctx.wait_q(eng, "S", "yS", 128)
            ctx.fence(eng.activation(silu_f[:], cvec[:], AF.Silu) if em else None, eng, "fS")
            i = eng.copy(silu_b[:], silu_f[:]) if em else None
            ctx.inc(i, "sS", mark=("silu",))
            ctx.fence(eng.activation(gsc[:], tdc[:], AF.Exp) if em else None, eng, "fS")
            ctx.fence(eng.activation(dcol[:], gsc[:], AF.Exp, scale=-1.0) if em else None, eng, "fS")
            i = eng.copy(dcolb[:], dcol[:]) if em else None
            ctx.inc(i, "sS", mark=("d",))

            def squares(sh, base):
                if base is not None:
                    ctx.wait_q(eng, "S", base, 64)
                for i2 in range(NT):
                    ins = None
                    if em:
                        ins = eng.activation(sqd[:, 0:1024], src_tile(sh, i2), AF.Square,
                                             accum_out=ssq[:, i2:i2 + 1])
                    ctx.inc(ins, "sS", mark=("sq", sh, i2))

            def sqrt_step(sh, prev=None):
                ctx.wait(eng, "S", ("varred", sh))
                if prev is not None:
                    ctx.wait(eng, "S", ("stat", prev))
                i2 = None
                if em:
                    i2 = eng.activation(sds[:], ssq[:], AF.Sqrt)
                ctx.inc(i2, "sS", mark=("sd", sh))

            squares("p", "yP")
            sqrt_step("p")
            squares("m", "yX")
            sqrt_step("m", prev="p")

            for c in range(NCH):
                ctx.wait(eng, "S", ("scan", "m", c))
                i = None
                if em:
                    eng.mul(t1s[:], p_scr[c % 2][:, 0:2048], tmv1[:, c:c + 1])
                    i = eng.mul(t2s[:], p_scr[c % 2][:, 0:2048], tmr1[:, c:c + 1])
                ctx.inc(i, "sS", mark=("mixS", c))

            for tt in range(NT):
                ctx.wait(eng, "S", ("wr", tt))
                i = None
                if em:
                    i = eng.activation(r_sig[:, tt * 1024:(tt + 1) * 1024],
                                       PA[tt % 2][:, 0:1024], AF.Sigmoid)
                ctx.inc(i, "sS", mark=("sigm", tt))

            ctx.wait(eng, "S", ("ep", NT - 1))
            squares("2", None)
            sqrt_step("2")

            for tt in range(NT):
                ctx.wait(eng, "S", ("lgprep", tt))
                ctx.fence(eng.activation(gc1[:, 3:4], gc1[:, 2:3], AF.Exp, bias=gc1[:, 1:2]) if em else None, eng, "fS")
                i = eng.activation(lg[:], lg[:], AF.Exp, bias=gc1[:, 1:2]) if em else None
                ctx.inc(i, "sS", mark=("gexp", tt))

            ctx.wait_q(eng, "S", "yL", 32)
            for u in range(NU):
                for tc in range(2):
                    for ic in range(8):
                        ctx.wait(eng, "S", ("h1", u, tc, ic))
                        if tc > 0 or u > 0:
                            pu, ptc = (u, 0) if tc > 0 else (u - 1, 1)
                            ctx.wait(eng, "S", ("w2", pu, ptc, 7))
                        i = None
                        if em:
                            i = eng.activation(
                                h1s[:, ic * 1024:(ic + 1) * 1024],
                                PA[ic % 2][:, 0:1024], AF.Gelu,
                                bias=b1c[:, u * 8 + ic:u * 8 + ic + 1])
                        ctx.inc(i, "sS", mark=("gelu", u, tc, ic))

        ctx.begin(False)
        emit_Y(None)
        emit_P(None)
        emit_T(None)
        emit_V(None)
        emit_S(None)
        counts0 = dict(ctx.counts)

        ctx.begin(True)
        with nc.Block() as block:
            block.sync(emit_Y)
            block.gpsimd(emit_P)
            block.tensor(emit_T)
            block.vector(emit_V)
            block.scalar(emit_S)
        assert ctx.counts == counts0, (ctx.counts, counts0)

    return nc


_BF = ml_dtypes.bfloat16


def kernel(**inputs):
    x = np.asarray(inputs["L0_attn_x"], dtype=np.float32)
    td = np.asarray(inputs["L0_attn_time_decay"], dtype=np.float32)
    tmv = np.asarray(inputs["L0_attn_time_mix_v"], dtype=np.float32)
    tmr = np.asarray(inputs["L0_attn_time_mix_r"], dtype=np.float32)
    Wv = np.asarray(inputs["L0_attn_Wv"], dtype=np.float32)
    Wr = np.asarray(inputs["L0_attn_Wr"], dtype=np.float32)
    Wo = np.asarray(inputs["L0_attn_Wo"], dtype=np.float32)
    Wg = np.asarray(inputs["L0_ffn_Wg"], dtype=np.float32)
    W1 = np.asarray(inputs["L0_ffn_W1"], dtype=np.float32)
    b1 = np.asarray(inputs["L0_ffn_b1"], dtype=np.float32)
    W2 = np.asarray(inputs["L0_ffn_W2"], dtype=np.float32)
    b2 = np.asarray(inputs["L0_ffn_b2"], dtype=np.float32)
    cc = np.asarray(inputs["L0_c"], dtype=np.float32)
    wmod = np.asarray(inputs["L0_wmod"], dtype=np.float32)

    WvT = np.ascontiguousarray(Wv.T)
    WrT = np.ascontiguousarray(Wr.T)
    WoT = np.ascontiguousarray(Wo.T)
    Wg_l = np.ascontiguousarray(Wg.reshape(8, 128, 8).transpose(1, 0, 2).reshape(128, 64))
    b1_cm = np.ascontiguousarray(b1.reshape(8, 16, 128).transpose(2, 0, 1).reshape(128, 128))
    wmod_c = np.ascontiguousarray(wmod)
    identf = np.eye(128, dtype=np.float32)
    identb = np.eye(128, dtype=_BF)
    onescol = np.ones((1, 128), dtype=_BF)
    zeros_xp = np.zeros((TPC, H), dtype=np.float32)

    def cm(v):
        return np.ascontiguousarray(v.reshape(8, 128).T)

    td_cm, tmv_cm, tmr_cm = cm(td), cm(tmv), cm(tmr)

    in_maps = []
    for core in range(8):
        b, half = core % 4, core // 4
        in_maps.append({
            "x": np.ascontiguousarray(x[b, half * TPC:(half + 1) * TPC]),
            "xp": (np.ascontiguousarray(x[b, 0:TPC]) if half else zeros_xp),
            "flag": np.full((128, 1), float(half), dtype=np.float32),
            "c_cm": cm(cc[b]),
            "td_cm": td_cm, "tmv_cm": tmv_cm, "tmr_cm": tmr_cm,
            "wmod": wmod_c, "WvT": WvT, "WrT": WrT, "WoT": WoT,
            "Wg_l": Wg_l, "W1": W1, "b1_cm": b1_cm, "W2": W2, "b2": b2,
            "identf": identf, "identb": identb, "onescol": onescol,
        })

    nc = build_nc()
    res = run_bass_kernel_spmd(nc, in_maps, core_ids=list(range(8)))
    out = np.empty((B, SEQ, H), dtype=np.float32)
    for core in range(8):
        b, half = core % 4, core // 4
        out[b, half * TPC:(half + 1) * TPC] = res.results[core]["out"]
    return out


# version 05
# speedup vs baseline: 1.0224x; 1.0224x over previous
import contextlib
import numpy as np
import ml_dtypes
import concourse.bass as bass
import concourse.mybir as mybir
from concourse.bass_utils import run_bass_kernel_spmd

F32 = mybir.dt.float32
BF16 = mybir.dt.bfloat16
AF = mybir.ActivationFunctionType
ALU = mybir.AluOpType

B, SEQ, H = 4, 4096, 1024
TPC = 2048
NT = 16
NCH = 8
NE, NI = 8, 2048
NU = 16
EPS = 1e-5
SHARDS = ["p", "m", "2"]


class Ctx:

    def __init__(self):
        self.emitting = False
        self.counts = {}
        self.marks = {}
        self.sems = {}
        self._waited = {}

    def begin(self, emitting):
        self.emitting = emitting
        self.counts = {}
        self._waited = {}

    def inc(self, inst, sem, n=1, mark=None):
        c = self.counts.get(sem, 0) + n
        self.counts[sem] = c
        if self.emitting:
            assert inst is not None, f"missing inst for inc {sem} {mark}"
            inst.then_inc(self.sems[sem], n)
            if mark is not None:
                assert self.marks[mark] == (sem, c), (mark, self.marks[mark], sem, c)
        else:
            if mark is not None:
                assert mark not in self.marks, mark
                self.marks[mark] = (sem, c)
        return c

    def alias(self, new, old):
        if not self.emitting:
            self.marks[new] = self.marks[old]

    def fence(self, inst, eng, sem):
        c = self.inc(inst, sem)
        if self.emitting:
            eng.wait_ge(self.sems[sem], c)

    def wait(self, eng, ename, mark):
        if not self.emitting:
            return
        sem, v = self.marks[mark]
        self._do_wait(eng, ename, sem, v)

    def wait_q(self, eng, ename, sem, v):
        if not self.emitting:
            return
        self._do_wait(eng, ename, sem, v)

    def _do_wait(self, eng, ename, sem, v):
        if v <= 0:
            return
        key = (ename, sem)
        if self._waited.get(key, -1) >= v:
            return
        self._waited[key] = v
        eng.wait_ge(self.sems[sem], v)


def build_nc(dbg=False):
    nc = bass.Bass("TRN2", target_bir_lowering=False, debug=False,
                   detect_race_conditions=False)

    d_x = nc.dram_tensor("x", [TPC, H], F32, kind="ExternalInput").ap()
    d_xp = nc.dram_tensor("xp", [TPC, H], F32, kind="ExternalInput").ap()
    d_vecf = nc.dram_tensor("vecs_f32", [128, 33], F32, kind="ExternalInput").ap()
    d_wmod = nc.dram_tensor("wmod", [H, 6 * H], F32, kind="ExternalInput").ap()
    d_WvT = nc.dram_tensor("WvT", [H, H], F32, kind="ExternalInput").ap()
    d_WrT = nc.dram_tensor("WrT", [H, H], F32, kind="ExternalInput").ap()
    d_WoT = nc.dram_tensor("WoT", [H, H], F32, kind="ExternalInput").ap()
    d_Wg = nc.dram_tensor("Wg_l", [128, 64], F32, kind="ExternalInput").ap()
    d_W1 = nc.dram_tensor("W1", [NE, H, NI], F32, kind="ExternalInput").ap()
    d_b1 = nc.dram_tensor("b1_cm", [128, 128], F32, kind="ExternalInput").ap()
    d_W2 = nc.dram_tensor("W2", [NE, NI, H], F32, kind="ExternalInput").ap()
    d_b2 = nc.dram_tensor("b2", [8, H], F32, kind="ExternalInput").ap()
    d_idf = nc.dram_tensor("identf", [128, 128], F32, kind="ExternalInput").ap()
    d_idb = nc.dram_tensor("identb", [128, 128], BF16, kind="ExternalInput").ap()
    d_ones = nc.dram_tensor("onescol", [1, 128], BF16, kind="ExternalInput").ap()
    d_out = nc.dram_tensor("out", [TPC, H], F32, kind="ExternalOutput").ap()
    d_dbg = (nc.dram_tensor("dbg", [128, 16640], F32, kind="ExternalOutput").ap()
             if dbg else None)

    ctx = Ctx()

    with contextlib.ExitStack() as stk:
        E = stk.enter_context

        xt = E(nc.sbuf_tensor("xt", [128, 16384], F32))
        ARW = E(nc.sbuf_tensor("ARW", [128, 16384], F32))
        WA = E(nc.sbuf_tensor("WA", [128, 16384], BF16))
        SCR = E(nc.sbuf_tensor("SCR", [128, 17408], BF16))
        gb_bc = E(nc.sbuf_tensor("gb_bc", [128, 1024], BF16))
        xnscr = E(nc.sbuf_tensor("xnscr", [128, 2048], BF16))
        vecf = E(nc.sbuf_tensor("vecf", [128, 33], F32))
        cvec = vecf[:, 0:8]
        tdc = vecf[:, 8:16]
        tmv = vecf[:, 16:24]
        tmr = vecf[:, 24:32]
        flagc = vecf[:, 32:33]
        tmv1 = E(nc.sbuf_tensor("tmv1", [128, 8], F32))
        tmr1 = E(nc.sbuf_tensor("tmr1", [128, 8], F32))
        dcol = E(nc.sbuf_tensor("dcol", [128, 8], F32))
        dcolb = E(nc.sbuf_tensor("dcolb", [128, 8], BF16))
        silu_f = E(nc.sbuf_tensor("silu_f", [128, 8], F32))
        silu_b = E(nc.sbuf_tensor("silu_b", [128, 8], BF16))
        carryP = E(nc.sbuf_tensor("carryP", [128, 8], F32))
        modc = E(nc.sbuf_tensor("modc", [128, 48], F32))
        mscr = E(nc.sbuf_tensor("mscr", [128, 16], F32))
        garow2 = E(nc.sbuf_tensor("garow2", [1, 2048], BF16))
        ssum = E(nc.sbuf_tensor("ssum", [128, 16], F32))
        ssq = E(nc.sbuf_tensor("ssq", [128, 16], F32))
        sds = E(nc.sbuf_tensor("sds", [128, 16], F32))
        srstd = E(nc.sbuf_tensor("srstd", [128, 16], F32))
        snmr = E(nc.sbuf_tensor("snmr", [128, 16], F32))
        wg_sb = E(nc.sbuf_tensor("wg_sb", [128, 64], BF16))
        identf = E(nc.sbuf_tensor("idf_s", [128, 128], F32))
        identb = E(nc.sbuf_tensor("idb_s", [128, 128], BF16))
        ones_c = E(nc.sbuf_tensor("ones_c", [1, 128], BF16))
        lg = E(nc.sbuf_tensor("lg", [128, 128], F32))
        lgB = E(nc.sbuf_tensor("lgB", [128, 128], F32))
        gsc = E(nc.sbuf_tensor("gsc", [128, 8], F32))
        gc1 = E(nc.sbuf_tensor("gc1", [128, 4], F32))
        wt = E(nc.sbuf_tensor("wt", [128, 128], F32))

        PA = [E(nc.psum_tensor(f"PA{i}", [128, 1024], F32)) for i in range(4)]
        PAb = [p[:].bitcast(BF16) for p in PA]

        for s in ["yS", "yX", "yP", "yL", "pW0", "pW1", "pG", "pR",
                  "pWr", "pWv", "pWo", "pU0", "pU1",
                  "sT", "sS", "sV", "sO", "fV", "fS"]:
            ctx.sems[s] = E(nc.semaphore(s))

        ARWb = ARW[:].bitcast(BF16)
        vin = ARWb[:, 0:16384]
        rin = ARWb[:, 16384:32768]
        v_scr = ARWb[:, 16384:24576]
        uslot = [
            (ARWb[:, 24576:32768], ARWb[:, 16384:24576]),
            (ARWb[:, 0:8192], ARWb[:, 8192:16384]),
        ]
        xp_f = ARW

        wmod_sl = [WA[:, 0:6144], WA[:, 6144:12288]]
        WrT_s = WA[:, 0:8192]
        WvT_s = WA[:, 8192:16384]
        WoT_s = WA[:, 0:8192]
        h1s = WA[:, 0:8192]
        b1c = WA[:, 8192:8448].bitcast(F32)
        b2_sb = WA[0:8, 8448:10496].bitcast(F32)
        b2g = WA[0:8, 10496:11520]
        wTb = WA[0:8, 11520:13568]

        r_sig = SCR[:, 0:16384]
        h_scr = [SCR[:, 0:2048], SCR[:, 2048:4096]]
        p_scr = [SCR[:, 4096:6144], SCR[:, 6144:8192]]
        t1s = SCR[:, 8192:10240]
        t2s = SCR[:, 10240:12288]
        sqd = SCR[:, 12288:14336].bitcast(F32)
        h_bT = SCR[:, 0:16384]
        tmpb = SCR[:, 16384:17408]
        ga_bc = tmpb

        def src_tile(sh, i):
            return (xp_f if sh == "p" else xt)[:, i * 1024:(i + 1) * 1024]

        def gidx(sh, c, g):
            return (SHARDS.index(sh) * NCH + c) * 2 + g

        def emit_Y(eng):
            em = ctx.emitting
            for dst, srcap in [(vecf[:], d_vecf), (identf[:], d_idf),
                               (identb[:], d_idb), (ones_c[:], d_ones)]:
                i = None
                if em:
                    i = eng.dma_start(out=dst, in_=srcap)
                ctx.inc(i, "yS", 16)
            ctx.wait(eng, "Y", ("wv_done",))
            for dst, src in [(b1c, d_b1), (b2_sb, d_b2)]:
                i = None
                if em:
                    i = eng.dma_start(out=dst, in_=src)
                ctx.inc(i, "yL", 16)
            for g in range(16):
                ctx.wait(eng, "Y", ("evac", 15, g // 8, g % 8))
                i = None
                if em:
                    i = eng.dma_start(out=d_out[g * 128:(g + 1) * 128, :],
                                      in_=xt[:, g * 1024:(g + 1) * 1024])
                ctx.inc(i, "sO", 16)
            if dbg:
                for dst, srcap in [(d_dbg[:, 0:48], modc[:]),
                                   (d_dbg[:, 48:56], carryP[:]),
                                   (d_dbg[:, 56:184], wt[:]),
                                   (d_dbg[:, 256:8448], SCR[:, 0:16384].bitcast(F32))]:
                    i = None
                    if em:
                        i = eng.dma_start(out=dst, in_=srcap)
                    ctx.inc(i, "sO", 16)
            ctx.wait_q(eng, "Y", "sO", 16 * 16 + (64 if dbg else 0))

        def emit_P(eng):
            em = ctx.emitting

            def dma(sem, dst, src):
                i = None
                if em:
                    i = eng.dma_start(out=dst, in_=src)
                ctx.inc(i, sem, 16)

            for hc in range(8):
                if hc >= 2:
                    ctx.wait(eng, "P", ("modchunk", hc - 2))
                dma("pW%d" % (hc % 2), wmod_sl[hc % 2], d_wmod[hc * 128:(hc + 1) * 128, :])
            dma("pG", wg_sb[:], d_Wg)
            ctx.wait(eng, "P", ("modc_tf",))
            for cs, off in [(16, 0), (40, 1024)]:
                for j in range(8):
                    dma("pR", garow2[0:1, off + j * 128:off + (j + 1) * 128],
                        modc[:, cs + j:cs + j + 1])
            dma("pWr", WrT_s.rearrange("p (c f) -> p c f", f=1024),
                d_WrT.rearrange("(c p) f -> p c f", p=128))
            dma("pWv", WvT_s.rearrange("p (c f) -> p c f", f=1024),
                d_WvT.rearrange("(c p) f -> p c f", p=128))
            ctx.wait(eng, "P", ("wr_done",))
            dma("pWo", WoT_s.rearrange("p (c f) -> p c f", f=1024),
                d_WoT.rearrange("(c p) f -> p c f", p=128))
            for u in range(NU):
                e, ih = u // 2, u % 2
                w1d, w2d = uslot[u % 2]
                if u == 1:
                    ctx.wait(eng, "P", ("wv_done",))
                elif u >= 2:
                    ctx.wait(eng, "P", ("w2", u - 2, 1, 7))
                dma("pU%d" % (u % 2), w1d.rearrange("p (c f) -> p c f", f=1024),
                    d_W1[e, :, ih * 1024:(ih + 1) * 1024].rearrange("(c p) f -> p c f", p=128))
                if u == 0:
                    ctx.wait(eng, "P", ("wo_done",))
                dma("pU%d" % (u % 2), w2d.rearrange("p (c f) -> p c f", f=1024),
                    d_W2[e, ih * 1024:(ih + 1) * 1024, :].rearrange("(c p) f -> p c f", p=128))

        def emit_T(eng):
            em = ctx.emitting

            ctx.wait(eng, "T", ("silu",))
            for hc in range(8):
                ctx.wait_q(eng, "T", "pW%d" % (hc % 2), 16 * (hc // 2 + 1))
                i = None
                for oc in range(48):
                    if em:
                        i = eng.matmul(PA[3][:, oc:oc + 1],
                                       wmod_sl[hc % 2][:, oc * 128:(oc + 1) * 128],
                                       silu_b[:, hc:hc + 1],
                                       start=(hc == 0 and oc == 0),
                                       stop=(hc == 7 and oc == 47),
                                       skip_group_check=True)
                ctx.inc(i, "sT", mark=("modchunk", hc))
            ctx.alias(("mod_done",), ("modchunk", 7))

            ctx.wait_q(eng, "T", "pR", 256)
            i = None
            if em:
                for j in range(8):
                    eng.matmul(PA[2][:, j * 128:(j + 1) * 128], ones_c[:],
                               garow2[0:1, j * 128:(j + 1) * 128], start=True, stop=True)
                for j in range(8):
                    i = eng.matmul(PA[1][:, j * 128:(j + 1) * 128], ones_c[:],
                                   garow2[0:1, 1024 + j * 128:1024 + (j + 1) * 128],
                                   start=True, stop=True)
            ctx.inc(i, "sT", mark=("gabc_ps",))

            ctx.wait_q(eng, "T", "yS", 64)
            for sh in SHARDS:
                for c in range(NCH):
                    for g in range(2):
                        gi = gidx(sh, c, g)
                        ctx.wait(eng, "T", ("xn", sh, c, g))
                        if gi >= 2:
                            pg = gi - 2
                            psh, pr = divmod(pg, NCH * 2)
                            ctx.wait(eng, "T", ("trev", SHARDS[psh], pr // 2, pr % 2))
                        else:
                            ctx.wait(eng, "T", ("gabc",))
                        i = None
                        if em:
                            for j in range(8):
                                i = eng.transpose(
                                    PAb[g % 2][:, j * 128:(j + 1) * 128],
                                    xnscr[:, (g % 2) * 1024 + j * 128:(g % 2) * 1024 + (j + 1) * 128],
                                    identb[:])
                        ctx.inc(i, "sT", mark=("tr", sh, c, g))
                if sh == "m":
                    emit_T_attn(eng)
                elif sh == "2":
                    emit_T_gate_ffn(eng)

        def emit_T_attn(eng):
            em = ctx.emitting
            ctx.wait(eng, "T", ("mix", 7))
            ctx.wait_q(eng, "T", "pWr", 16)
            for tt in range(NT):
                if tt >= 2:
                    ctx.wait(eng, "T", ("sigm", tt - 2))
                i = None
                if em:
                    for hc in range(8):
                        for nh in range(2):
                            i = eng.matmul(
                                PA[tt % 2][:, nh * 512:(nh + 1) * 512],
                                rin[:, hc * 2048 + tt * 128:hc * 2048 + (tt + 1) * 128],
                                WrT_s[:, hc * 1024 + nh * 512:hc * 1024 + (nh + 1) * 512],
                                start=(hc == 0), stop=(hc == 7))
                ctx.inc(i, "sT", mark=("wr", tt))
            ctx.alias(("wr_done",), ("wr", NT - 1))

            ctx.wait_q(eng, "T", "pWv", 16)
            ctx.wait(eng, "T", ("wotg",))
            for t4 in range(4):
                for oc in range(8):
                    gi = t4 * 8 + oc
                    if gi >= 2:
                        ctx.wait(eng, "T", ("vev", (gi - 2) // 8, (gi - 2) % 8))
                    i = None
                    if em:
                        for hc in range(8):
                            i = eng.matmul(
                                PA[2][:, (oc % 2) * 512:(oc % 2 + 1) * 512],
                                WvT_s[:, hc * 1024 + oc * 128:hc * 1024 + (oc + 1) * 128],
                                vin[:, hc * 2048 + t4 * 512:hc * 2048 + (t4 + 1) * 512],
                                start=(hc == 0), stop=(hc == 7))
                    ctx.inc(i, "sT", mark=("wv", t4, oc))
                if t4 == 3:
                    ctx.alias(("wv_done",), ("wv", 3, 7))
                for ti in range(4):
                    tt = t4 * 4 + ti
                    ctx.wait(eng, "T", ("vev", t4, 7))
                    if tt >= 2:
                        ctx.wait(eng, "T", ("ep", tt - 2))
                    else:
                        ctx.wait(eng, "T", ("sigm", NT - 2 + tt))
                    i = None
                    if em:
                        for oc in range(8):
                            for nh in range(2):
                                i = eng.matmul(
                                    PA[tt % 2][:, nh * 512:(nh + 1) * 512],
                                    v_scr[:, oc * 512 + ti * 128:oc * 512 + (ti + 1) * 128],
                                    WoT_s[:, oc * 1024 + nh * 512:oc * 1024 + (nh + 1) * 512],
                                    start=(oc == 0), stop=(oc == 7))
                    ctx.inc(i, "sT", mark=("wo", tt))
            ctx.alias(("wo_done",), ("wo", NT - 1))

        def emit_T_gate_ffn(eng):
            em = ctx.emitting
            ctx.wait(eng, "T", ("trev", "2", NCH - 1, 1))
            ctx.wait_q(eng, "T", "pG", 16)
            i = None
            if em:
                for tt in range(NT):
                    for hc in range(8):
                        i = eng.matmul(
                            PA[3][:, tt * 8:(tt + 1) * 8],
                            h_bT[:, hc * 2048 + tt * 128:hc * 2048 + (tt + 1) * 128],
                            wg_sb[:, hc * 8:(hc + 1) * 8],
                            start=(tt == 0 and hc == 0), stop=(tt == 15 and hc == 7),
                            skip_group_check=True)
            ctx.inc(i, "sT", mark=("lgall",))
            ctx.wait(eng, "T", ("wall",))
            for k in range(4):
                if k >= 1:
                    ctx.wait(eng, "T", ("wTbb", k - 1))
                i = None
                if em:
                    for j in range(4):
                        tt = k * 4 + j
                        i = eng.transpose(PA[3][0:8, 512 + j * 128:512 + (j + 1) * 128],
                                          wt[:, tt * 8:(tt + 1) * 8], identf[:])
                ctx.inc(i, "sT", mark=("wtrb", k))

            for u in range(NU):
                w1s, w2s = uslot[u % 2]
                ctx.wait(eng, "T", ("ufold", u))
                for tc in range(2):
                    for ic in range(8):
                        if ic >= 2:
                            ctx.wait(eng, "T", ("gelu", u, tc, ic - 2))
                        elif tc > 0 or u > 0:
                            pu, ptc = (u, tc - 1) if tc > 0 else (u - 1, 1)
                            ctx.wait(eng, "T", ("gelu", pu, ptc, ic + 6))
                        i = None
                        if em:
                            for hc in range(8):
                                for nh in range(2):
                                    i = eng.matmul(
                                        PA[ic % 2][:, nh * 512:(nh + 1) * 512],
                                        w1s[:, hc * 1024 + ic * 128:hc * 1024 + (ic + 1) * 128],
                                        h_bT[:, hc * 2048 + tc * 1024 + nh * 512:
                                             hc * 2048 + tc * 1024 + (nh + 1) * 512],
                                        start=(hc == 0), stop=(hc == 7))
                        ctx.inc(i, "sT", mark=("h1", u, tc, ic))
                    ctx.wait(eng, "T", ("gelu", u, tc, 7))
                    for ts in range(8):
                        w2g = (u * 2 + tc) * 8 + ts
                        if w2g >= 2:
                            pg = w2g - 2
                            ctx.wait(eng, "T", ("evac", pg // 16, (pg % 16) // 8, pg % 8))
                        i = None
                        if em:
                            if u == 0:
                                for nh in range(2):
                                    eng.matmul(
                                        PA[2 + ts % 2][:, nh * 512:(nh + 1) * 512],
                                        wTb[:, (tc * 8 + ts) * 128:(tc * 8 + ts + 1) * 128],
                                        b2g[:, nh * 512:(nh + 1) * 512],
                                        start=True, stop=False)
                            for ic in range(8):
                                for nh in range(2):
                                    i = eng.matmul(
                                        PA[2 + ts % 2][:, nh * 512:(nh + 1) * 512],
                                        h1s[:, ic * 1024 + ts * 128:ic * 1024 + (ts + 1) * 128],
                                        w2s[:, ic * 1024 + nh * 512:ic * 1024 + (nh + 1) * 512],
                                        start=(u != 0 and ic == 0), stop=(ic == 7))
                        ctx.inc(i, "sT", mark=("w2", u, tc, ts))

        def emit_V(eng):
            em = ctx.emitting
            ctx.wait_q(eng, "V", "yS", 64)
            if em:
                eng.tensor_scalar(tmv1[:], tmv[:], -1.0, 1.0, ALU.mult, ALU.add)
                eng.tensor_scalar(tmr1[:], tmr[:], -1.0, 1.0, ALU.mult, ALU.add)

            ctx.wait(eng, "V", ("mod_done",))
            i = None
            ctx.fence(eng.tensor_copy(modc[:], PA[3][:, 0:48]) if em else None, eng, "fV")
            if em:
                eng.tensor_scalar(modc[:, 8:16], modc[:, 8:16], 1.0, None, ALU.add)
                eng.tensor_scalar(modc[:, 32:40], modc[:, 32:40], 1.0, None, ALU.add)
            for cs in (16, 40):
                ctx.fence(eng.tensor_scalar(mscr[:, 0:8], modc[:, cs:cs + 8], -1.0, None, ALU.mult) if em else None, eng, "fV")
                ctx.fence(eng.tensor_tensor(mscr[:, 0:8], modc[:, cs:cs + 8], mscr[:, 0:8], ALU.max) if em else None, eng, "fV")
                ctx.fence(eng.tensor_scalar(mscr[:, 0:8], mscr[:, 0:8], 0.25, None, ALU.max) if em else None, eng, "fV")
                ctx.fence(eng.tensor_scalar(mscr[:, 8:16], modc[:, cs:cs + 8], 0.0, None, ALU.is_ge) if em else None, eng, "fV")
                ctx.fence(eng.tensor_scalar(mscr[:, 8:16], mscr[:, 8:16], 2.0, -1.0, ALU.mult, ALU.add) if em else None, eng, "fV")
                i = eng.tensor_tensor(modc[:, cs:cs + 8], mscr[:, 0:8], mscr[:, 8:16], ALU.mult) if em else None
            ctx.inc(i, "sV", mark=("modc_tf",))
            ctx.wait(eng, "V", ("gabc_ps",))
            i = None
            if em:
                eng.tensor_copy(ga_bc[:, 0:1024], PA[2][:, 0:1024])
                i = eng.tensor_copy(gb_bc[:], PA[1][:, 0:1024])
            ctx.inc(i, "sV", mark=("gabc",))

            for sh in ["p", "m"]:
                ctx.wait_q(eng, "V", "yP" if sh == "p" else "yX", 64)
                for i2 in range(NT):
                    if em:
                        eng.reduce_sum(ssum[:, i2:i2 + 1], src_tile(sh, i2),
                                       mybir.AxisListType.X)
                ctx.wait(eng, "V", ("sq", sh, NT - 1))
                ctx.fence(eng.tensor_scalar(ssum[:], ssum[:], 1.0 / 1024, None, ALU.mult) if em else None, eng, "fV")
                ctx.fence(eng.tensor_scalar(ssq[:], ssq[:], 1.0 / 1024, None, ALU.mult) if em else None, eng, "fV")
                ctx.fence(eng.tensor_tensor(snmr[:], ssum[:], ssum[:], ALU.mult) if em else None, eng, "fV")
                ctx.fence(eng.tensor_tensor(ssq[:], ssq[:], snmr[:], ALU.subtract) if em else None, eng, "fV")
                i = eng.tensor_scalar(ssq[:], ssq[:], EPS, None, ALU.add) if em else None
                ctx.inc(i, "sV", mark=("varred", sh))
                ctx.wait(eng, "V", ("sd", sh))
                ctx.fence(eng.reciprocal(srstd[:], sds[:]) if em else None, eng, "fV")
                ctx.fence(eng.tensor_tensor(snmr[:], ssum[:], srstd[:], ALU.mult) if em else None, eng, "fV")
                i = eng.tensor_scalar(snmr[:], snmr[:], -1.0, None, ALU.mult) if em else None
                ctx.inc(i, "sV", mark=("stat", sh))
                ctx.fence(None if not em else eng.tensor_copy(gc1[:, 0:1], snmr[:, 0:1]), eng, "fV")

                for c in range(NCH):
                    for g in range(2):
                        gi = gidx(sh, c, g)
                        if gi >= 2:
                            pg = gi - 2
                            psh, pr = divmod(pg, NCH * 2)
                            ctx.wait(eng, "V", ("tr", SHARDS[psh], pr // 2, pr % 2))
                        i = None
                        if em:
                            for j in range(8):
                                ti = g * 8 + j
                                i = eng.tensor_scalar(
                                    xnscr[:, (g % 2) * 1024 + j * 128:(g % 2) * 1024 + (j + 1) * 128],
                                    src_tile(sh, ti)[:, c * 128:(c + 1) * 128],
                                    srstd[:, ti:ti + 1], snmr[:, ti:ti + 1],
                                    ALU.mult, ALU.add)
                        ctx.inc(i, "sV", mark=("xn", sh, c, g))
                        ctx.wait(eng, "V", ("tr", sh, c, g))
                        ccol, scol = (8, 0)
                        i = None
                        if em:
                            i = eng.tensor_scalar(
                                h_scr[c % 2][:, g * 1024:(g + 1) * 1024],
                                PAb[g % 2][:, 0:1024],
                                modc[:, ccol + c:ccol + c + 1],
                                modc[:, scol + c:scol + c + 1],
                                ALU.mult, ALU.add)
                        ctx.inc(i, "sV", mark=("trev", sh, c, g))
                    if sh == "m" and c >= 2:
                        ctx.wait(eng, "V", ("mixS", c - 2))
                    ctx.wait(eng, "V", ("d",))
                    i = None
                    if em:
                        if sh == "m":
                            eng.tensor_copy(p_scr[c % 2][:, 0:1], carryP[:, c:c + 1])
                            i = eng.tensor_tensor_scan(
                                p_scr[c % 2][:, 1:2048],
                                dcolb[:, c:c + 1].broadcast_to([128, 2047]),
                                h_scr[c % 2][:, 0:2047],
                                carryP[:, c:c + 1], ALU.mult, ALU.add)
                        else:
                            i = eng.tensor_tensor_scan(
                                p_scr[c % 2][:, 0:2048],
                                dcolb[:, c:c + 1].broadcast_to([128, 2048]),
                                h_scr[c % 2][:, 0:2048],
                                0.0, ALU.mult, ALU.add)
                    ctx.inc(i, "sV", mark=("scan", sh, c))
                    if sh == "p":
                        ctx.wait_q(eng, "V", "yS", 64)
                        ctx.fence(eng.engine_nop() if em else None, eng, "fV")
                        i = None
                        if em:
                            i = eng.tensor_scalar(carryP[:, c:c + 1],
                                                  p_scr[c % 2][:, 2047:2048],
                                                  flagc[:, 0:1], None, ALU.mult)
                        ctx.inc(i, "sV", mark=("carry", c))
                    else:
                        ctx.wait(eng, "V", ("mixS", c))
                        i = None
                        if em:
                            eng.scalar_tensor_tensor(
                                vin[:, c * 2048:(c + 1) * 2048], h_scr[c % 2],
                                tmv[:, c:c + 1], t1s[:], ALU.mult, ALU.add)
                            i = eng.scalar_tensor_tensor(
                                rin[:, c * 2048:(c + 1) * 2048], h_scr[c % 2],
                                tmr[:, c:c + 1], t2s[:], ALU.mult, ALU.add)
                        ctx.inc(i, "sV", mark=("mix", c))

            ctx.wait_q(eng, "V", "pWo", 16)
            i = None
            if em:
                for k in range(8):
                    i = eng.tensor_tensor(WoT_s[:, k * 1024:(k + 1) * 1024],
                                          WoT_s[:, k * 1024:(k + 1) * 1024],
                                          ga_bc[:, 0:1024], ALU.mult)
            ctx.inc(i, "sV", mark=("wotg",))

            for t4 in range(4):
                for oc in range(8):
                    ctx.wait(eng, "V", ("wv", t4, oc))
                    i = None
                    if em:
                        i = eng.tensor_copy(v_scr[:, oc * 512:(oc + 1) * 512],
                                            PA[2][:, (oc % 2) * 512:(oc % 2 + 1) * 512])
                    ctx.inc(i, "sV", mark=("vev", t4, oc))
                for ti in range(4):
                    tt = t4 * 4 + ti
                    ctx.wait(eng, "V", ("wo", tt))
                    ctx.wait(eng, "V", ("sigm", tt))
                    i = None
                    if em:
                        eng.tensor_tensor(tmpb[:, 0:1024], PA[tt % 2][:, 0:1024],
                                          r_sig[:, tt * 1024:(tt + 1) * 1024], ALU.mult)
                        i = eng.tensor_tensor(xt[:, tt * 1024:(tt + 1) * 1024],
                                              xt[:, tt * 1024:(tt + 1) * 1024],
                                              tmpb[:, 0:1024], ALU.add)
                    ctx.inc(i, "sV", mark=("ep", tt))
                    if em:
                        eng.reduce_sum(ssum[:, tt:tt + 1], xt[:, tt * 1024:(tt + 1) * 1024],
                                       mybir.AxisListType.X)

            ctx.wait(eng, "V", ("sq", "2", NT - 1))
            ctx.fence(eng.tensor_scalar(ssum[:], ssum[:], 1.0 / 1024, None, ALU.mult) if em else None, eng, "fV")
            ctx.fence(eng.tensor_scalar(ssq[:], ssq[:], 1.0 / 1024, None, ALU.mult) if em else None, eng, "fV")
            ctx.fence(eng.tensor_tensor(snmr[:], ssum[:], ssum[:], ALU.mult) if em else None, eng, "fV")
            ctx.fence(eng.tensor_tensor(ssq[:], ssq[:], snmr[:], ALU.subtract) if em else None, eng, "fV")
            i = eng.tensor_scalar(ssq[:], ssq[:], EPS, None, ALU.add) if em else None
            ctx.inc(i, "sV", mark=("varred", "2"))
            ctx.wait(eng, "V", ("sd", "2"))
            ctx.fence(eng.reciprocal(srstd[:], sds[:]) if em else None, eng, "fV")
            ctx.fence(eng.tensor_tensor(snmr[:], ssum[:], srstd[:], ALU.mult) if em else None, eng, "fV")
            i = eng.tensor_scalar(snmr[:], snmr[:], -1.0, None, ALU.mult) if em else None
            ctx.inc(i, "sV", mark=("stat", "2"))
            ctx.fence(None if not em else eng.tensor_copy(gc1[:, 0:1], snmr[:, 0:1]), eng, "fV")
            for c in range(NCH):
                for g in range(2):
                    gi = gidx("2", c, g)
                    pg = gi - 2
                    psh, pr = divmod(pg, NCH * 2)
                    ctx.wait(eng, "V", ("tr", SHARDS[psh], pr // 2, pr % 2))
                    i = None
                    if em:
                        for j in range(8):
                            ti = g * 8 + j
                            i = eng.tensor_scalar(
                                xnscr[:, (g % 2) * 1024 + j * 128:(g % 2) * 1024 + (j + 1) * 128],
                                xt[:, ti * 1024 + c * 128:ti * 1024 + c * 128 + 128],
                                srstd[:, ti:ti + 1], snmr[:, ti:ti + 1],
                                ALU.mult, ALU.add)
                    ctx.inc(i, "sV", mark=("xn", "2", c, g))
                    ctx.wait(eng, "V", ("tr", "2", c, g))
                    i = None
                    if em:
                        i = eng.tensor_scalar(
                            h_bT[:, c * 2048 + g * 1024:c * 2048 + (g + 1) * 1024],
                            PAb[g % 2][:, 0:1024],
                            modc[:, 32 + c:32 + c + 1], modc[:, 24 + c:24 + c + 1],
                            ALU.mult, ALU.add)
                    ctx.inc(i, "sV", mark=("trev", "2", c, g))

            ctx.wait_q(eng, "V", "yL", 32)
            i = None
            if em:
                i = eng.tensor_tensor(b2g[:], b2_sb, gb_bc[0:8, 0:1024], ALU.mult)
            ctx.inc(i, "sV", mark=("b2g",))
            ctx.wait(eng, "V", ("lgall",))
            lgA3 = lg[:].rearrange("p (t e) -> p t e", e=8)
            lgB3 = lgB[:].rearrange("p (t e) -> p t e", e=8)
            wt3 = wt[:].rearrange("p (t e) -> p t e", e=8)
            m1 = mscr[:, 0:16]
            m2 = ssum[:, 0:16]
            dm = ssq[:, 0:16]
            rden = sds[:, 0:16]

            def bc(col16):
                return col16.rearrange("p t -> p t ()").broadcast_to([128, 16, 8])

            ctx.fence(eng.tensor_copy(lg[:], PA[3][:, 0:128]) if em else None, eng, "fV")
            ctx.fence(eng.tensor_reduce(m1.rearrange("p t -> p t ()"), lgA3,
                                        mybir.AxisListType.X, ALU.max) if em else None, eng, "fV")
            ctx.fence(eng.tensor_tensor(lgB3, lgA3, bc(m1), ALU.subtract) if em else None, eng, "fV")
            ctx.fence(eng.tensor_tensor(wt3, lgA3, bc(m1), ALU.is_equal) if em else None, eng, "fV")
            ctx.fence(eng.scalar_tensor_tensor(wt[:], wt[:], -1e30, lg[:], ALU.mult, ALU.add) if em else None, eng, "fV")
            ctx.fence(eng.tensor_reduce(m2.rearrange("p t -> p t ()"), wt3,
                                        mybir.AxisListType.X, ALU.max) if em else None, eng, "fV")
            ctx.fence(eng.tensor_tensor(wt3, lgA3, bc(m2), ALU.is_ge) if em else None, eng, "fV")
            i = eng.tensor_tensor(dm, m2, m1, ALU.subtract) if em else None
            ctx.inc(i, "sV", mark=("lgprepall",))
            ctx.wait(eng, "V", ("gexpall",))
            ctx.fence(eng.tensor_scalar(rden, rden, 1.0, None, ALU.add) if em else None, eng, "fV")
            ctx.fence(eng.reciprocal(rden, rden) if em else None, eng, "fV")
            ctx.fence(eng.tensor_tensor(lg[:], lgB[:], wt[:], ALU.mult) if em else None, eng, "fV")
            i = eng.tensor_tensor(wt3, lgA3, bc(rden), ALU.mult) if em else None
            ctx.inc(i, "sV", mark=("wall",))
            for k in range(4):
                ctx.wait(eng, "V", ("wtrb", k))
                i = None
                if em:
                    i = eng.tensor_copy(wTb[:, k * 512:(k + 1) * 512],
                                        PA[3][0:8, 512:1024])
                ctx.inc(i, "sV", mark=("wTbb", k))

            for u in range(NU):
                w1s, w2s = uslot[u % 2]
                ctx.wait_q(eng, "V", "pU%d" % (u % 2), 32 * (u // 2 + 1))
                i = None
                if em:
                    for k in range(8):
                        i = eng.tensor_tensor(w2s[:, k * 1024:(k + 1) * 1024],
                                              w2s[:, k * 1024:(k + 1) * 1024],
                                              gb_bc[:, 0:1024], ALU.mult)
                ctx.inc(i, "sV", mark=("ufold", u))
                e = u // 2
                for tc in range(2):
                    for ts in range(8):
                        ctx.wait(eng, "V", ("w2", u, tc, ts))
                        g = tc * 8 + ts
                        i = None
                        if em:
                            i = eng.scalar_tensor_tensor(
                                xt[:, g * 1024:(g + 1) * 1024],
                                PA[2 + ts % 2][:, 0:1024],
                                wt[:, g * 8 + e:g * 8 + e + 1],
                                xt[:, g * 1024:(g + 1) * 1024],
                                ALU.mult, ALU.add)
                        ctx.inc(i, "sV", mark=("evac", u, tc, ts))

        def emit_S(eng):
            em = ctx.emitting
            for k in range(4):
                i = None
                if em:
                    i = eng.dma_start(
                        out=xt[:, k * 4096:(k + 1) * 4096].rearrange("p (i f) -> p i f", f=1024),
                        in_=d_x[k * 512:(k + 1) * 512, :].rearrange("(i p) f -> p i f", p=128))
                ctx.inc(i, "yX", 16)
            for k in range(4):
                i = None
                if em:
                    i = eng.dma_start(
                        out=xp_f[:, k * 4096:(k + 1) * 4096].rearrange("p (i f) -> p i f", f=1024),
                        in_=d_xp[k * 512:(k + 1) * 512, :].rearrange("(i p) f -> p i f", p=128))
                ctx.inc(i, "yP", 16)
            ctx.wait_q(eng, "S", "yS", 64)
            ctx.fence(eng.activation(silu_f[:], cvec[:], AF.Silu) if em else None, eng, "fS")
            i = eng.copy(silu_b[:], silu_f[:]) if em else None
            ctx.inc(i, "sS", mark=("silu",))
            ctx.fence(eng.activation(gsc[:], tdc[:], AF.Exp) if em else None, eng, "fS")
            ctx.fence(eng.activation(dcol[:], gsc[:], AF.Exp, scale=-1.0) if em else None, eng, "fS")
            i = eng.copy(dcolb[:], dcol[:]) if em else None
            ctx.inc(i, "sS", mark=("d",))

            def squares(sh, base, epwait=False):
                if base is not None:
                    ctx.wait_q(eng, "S", base, 64)
                for i2 in range(NT):
                    if epwait:
                        ctx.wait(eng, "S", ("ep", i2))
                    ins = None
                    if em:
                        ins = eng.activation(sqd[:, 0:1024], src_tile(sh, i2), AF.Square,
                                             accum_out=ssq[:, i2:i2 + 1])
                    ctx.inc(ins, "sS", mark=("sq", sh, i2))

            def sqrt_step(sh, prev=None):
                ctx.wait(eng, "S", ("varred", sh))
                if prev is not None:
                    ctx.wait(eng, "S", ("stat", prev))
                i2 = None
                if em:
                    i2 = eng.activation(sds[:], ssq[:], AF.Sqrt)
                ctx.inc(i2, "sS", mark=("sd", sh))

            squares("p", "yP")
            sqrt_step("p")
            squares("m", "yX")
            sqrt_step("m", prev="p")

            for c in range(NCH):
                ctx.wait(eng, "S", ("scan", "m", c))
                i = None
                if em:
                    eng.mul(t1s[:], p_scr[c % 2][:, 0:2048], tmv1[:, c:c + 1])
                    i = eng.mul(t2s[:], p_scr[c % 2][:, 0:2048], tmr1[:, c:c + 1])
                ctx.inc(i, "sS", mark=("mixS", c))

            for tt in range(NT):
                ctx.wait(eng, "S", ("wr", tt))
                i = None
                if em:
                    i = eng.activation(r_sig[:, tt * 1024:(tt + 1) * 1024],
                                       PA[tt % 2][:, 0:1024], AF.Sigmoid)
                ctx.inc(i, "sS", mark=("sigm", tt))

            ctx.wait(eng, "S", ("ep", NT - 1))
            squares("2", None)
            sqrt_step("2")

            ctx.wait(eng, "S", ("lgprepall",))
            ctx.fence(eng.activation(lgB[:], lgB[:], AF.Exp) if em else None, eng, "fS")
            i = eng.activation(sds[:, 0:16], ssq[:, 0:16], AF.Exp) if em else None
            ctx.inc(i, "sS", mark=("gexpall",))

            ctx.wait_q(eng, "S", "yL", 32)
            for u in range(NU):
                for tc in range(2):
                    for ic in range(8):
                        ctx.wait(eng, "S", ("h1", u, tc, ic))
                        if tc > 0 or u > 0:
                            pu, ptc = (u, 0) if tc > 0 else (u - 1, 1)
                            ctx.wait(eng, "S", ("w2", pu, ptc, 7))
                        i = None
                        if em:
                            i = eng.activation(
                                h1s[:, ic * 1024:(ic + 1) * 1024],
                                PA[ic % 2][:, 0:1024], AF.Gelu,
                                bias=b1c[:, u * 8 + ic:u * 8 + ic + 1])
                        ctx.inc(i, "sS", mark=("gelu", u, tc, ic))

        ctx.begin(False)
        emit_Y(None)
        emit_P(None)
        emit_T(None)
        emit_V(None)
        emit_S(None)
        counts0 = dict(ctx.counts)

        ctx.begin(True)
        with nc.Block() as block:
            block.sync(emit_Y)
            block.gpsimd(emit_P)
            block.tensor(emit_T)
            block.vector(emit_V)
            block.scalar(emit_S)
        assert ctx.counts == counts0, (ctx.counts, counts0)

    return nc


_BF = ml_dtypes.bfloat16


def kernel(**inputs):
    x = np.asarray(inputs["L0_attn_x"], dtype=np.float32)
    td = np.asarray(inputs["L0_attn_time_decay"], dtype=np.float32)
    tmv = np.asarray(inputs["L0_attn_time_mix_v"], dtype=np.float32)
    tmr = np.asarray(inputs["L0_attn_time_mix_r"], dtype=np.float32)
    Wv = np.asarray(inputs["L0_attn_Wv"], dtype=np.float32)
    Wr = np.asarray(inputs["L0_attn_Wr"], dtype=np.float32)
    Wo = np.asarray(inputs["L0_attn_Wo"], dtype=np.float32)
    Wg = np.asarray(inputs["L0_ffn_Wg"], dtype=np.float32)
    W1 = np.asarray(inputs["L0_ffn_W1"], dtype=np.float32)
    b1 = np.asarray(inputs["L0_ffn_b1"], dtype=np.float32)
    W2 = np.asarray(inputs["L0_ffn_W2"], dtype=np.float32)
    b2 = np.asarray(inputs["L0_ffn_b2"], dtype=np.float32)
    cc = np.asarray(inputs["L0_c"], dtype=np.float32)
    wmod = np.asarray(inputs["L0_wmod"], dtype=np.float32)

    WvT = np.ascontiguousarray(Wv.T)
    WrT = np.ascontiguousarray(Wr.T)
    WoT = np.ascontiguousarray(Wo.T)
    Wg_l = np.ascontiguousarray(Wg.reshape(8, 128, 8).transpose(1, 0, 2).reshape(128, 64))
    b1_cm = np.ascontiguousarray(b1.reshape(8, 16, 128).transpose(2, 0, 1).reshape(128, 128))
    wmod_c = np.ascontiguousarray(wmod)
    identf = np.eye(128, dtype=np.float32)
    identb = np.eye(128, dtype=_BF)
    onescol = np.ones((1, 128), dtype=_BF)
    zeros_xp = np.zeros((TPC, H), dtype=np.float32)

    def cm(v):
        return np.ascontiguousarray(v.reshape(8, 128).T)

    td_cm, tmv_cm, tmr_cm = cm(td), cm(tmv), cm(tmr)

    in_maps = []
    for core in range(8):
        b, half = core % 4, core // 4
        vecs = np.concatenate([cm(cc[b]), td_cm, tmv_cm, tmr_cm,
                               np.full((128, 1), float(half), dtype=np.float32)],
                              axis=1)
        in_maps.append({
            "x": np.ascontiguousarray(x[b, half * TPC:(half + 1) * TPC]),
            "xp": (np.ascontiguousarray(x[b, 0:TPC]) if half else zeros_xp),
            "vecs_f32": np.ascontiguousarray(vecs),
            "wmod": wmod_c, "WvT": WvT, "WrT": WrT, "WoT": WoT,
            "Wg_l": Wg_l, "W1": W1, "b1_cm": b1_cm, "W2": W2, "b2": b2,
            "identf": identf, "identb": identb, "onescol": onescol,
        })

    nc = build_nc()
    res = run_bass_kernel_spmd(nc, in_maps, core_ids=list(range(8)))
    out = np.empty((B, SEQ, H), dtype=np.float32)
    for core in range(8):
        b, half = core % 4, core // 4
        out[b, half * TPC:(half + 1) * TPC] = res.results[core]["out"]
    return out
